# version 01 (final)
# baseline (speedup 1.0000x reference)
import numpy as np
import ml_dtypes

import concourse.bass as bass
import concourse.mybir as mybir
from concourse.bass_utils import run_bass_kernel_spmd

F32 = mybir.dt.float32
BF16 = mybir.dt.bfloat16
F16 = mybir.dt.float16
AX = mybir.AxisListType.X
OP = mybir.AluOpType
ACT = mybir.ActivationFunctionType

bfloat16 = ml_dtypes.bfloat16

B, N, D = 2, 8192, 1024
DH = 4096
R = 2048
HALO = 64
W = R + 2 * HALO
WT = W // 128
DT = D // 128
JT = DH // 128
NCORE = 8
NCHUNK = N // 256
MAGIC = 1536.0
RELU_SCALE = 1.0 / 8192.0

PAD_F1ST = 3000
PAD_F2IN = 12000
PAD_F3G = 9000
PAD_TILE = 9000
PAD_AG = 700000
PAD_RS = 1500000
PAD_FIN = 200000


class Sem:
    def __init__(self, h):
        self.h = h
        self.n = 0

    def inc(self, x, by=1):
        self.n += by
        if hasattr(x, "then_inc"):
            return x.then_inc(self.h, by)
        return x.nop().then_inc(self.h, by)

    def dinc(self, instr):
        self.n += 16
        return instr.then_inc(self.h, 16)


def build_program(debug_taps=(), stop_after=None):
    nc = bass.Bass("TRN2", target_bir_lowering=False, detect_race_conditions=False)

    xw_d = nc.dram_tensor("xw", [W, D], F32, kind="ExternalInput").ap()
    rmask_d = nc.dram_tensor("rowmask", [128, WT], F32, kind="ExternalInput").ap()
    w1t_d = nc.dram_tensor("w1t", [D, DH], F32, kind="ExternalInput").ap()
    w2t_d = nc.dram_tensor("w2t", [DH, D], F32, kind="ExternalInput").ap()
    cw_d = nc.dram_tensor("cw", [128, JT * 3], F32, kind="ExternalInput").ap()
    cb_d = nc.dram_tensor("cb", [128, JT], F32, kind="ExternalInput").ap()
    wqh_d = nc.dram_tensor("wqh", [D, 128], F32, kind="ExternalInput").ap()
    wkh_d = nc.dram_tensor("wkh", [D, 128], F32, kind="ExternalInput").ap()
    wvh_d = nc.dram_tensor("wvh", [D, D], F32, kind="ExternalInput").ap()
    identb_d = nc.dram_tensor("identb", [128, 128], BF16, kind="ExternalInput").ap()
    maskA_d = nc.dram_tensor("maskA", [128, 256], BF16, kind="ExternalInput").ap()
    maskB_d = nc.dram_tensor("maskB", [128, 256], BF16, kind="ExternalInput").ap()
    out_d = nc.dram_tensor("out", [R, D], F32, kind="ExternalOutput").ap()

    ag_in = nc.dram_tensor("ag_in", [D, R], BF16, kind="Internal").ap()
    ag_out = nc.dram_tensor("ag_out", [NCORE, D, R], BF16, kind="Internal").ap()
    rs_in = nc.dram_tensor("rs_in", [NCORE, W, D], BF16, kind="Internal").ap()
    rs_out = nc.dram_tensor("rs_out", [W, D], BF16, kind="Internal").ap()
    w1q_sp = nc.dram_tensor("w1q_sp", [128, DT * DH], BF16, kind="Internal").ap()
    w2q_sp = nc.dram_tensor("w2q_sp", [128, JT * D], BF16, kind="Internal").ap()
    x1_sp = nc.dram_tensor("x1_sp", [128, WT * D], F32, kind="Internal").ap()
    h_sp = nc.dram_tensor("h_sp", [JT, 128, W], F32, kind="Internal").ap()
    mw_sp = nc.dram_tensor("mw_sp", [6, 128], BF16, kind="Internal").ap()
    mwd = nc.dram_tensor("mwd", [1, 4], F32, kind="Internal").ap()
    bc_sp = nc.dram_tensor("bc_sp", [17, 384], BF16, kind="Internal").ap()

    dbg = {}
    for tap, shape, dt_ in (("x1", [128, WT * D], F32), ("y2t", [D, R], BF16),
                            ("hout", [W, D], BF16), ("x2", [128, WT * D], F32),
                            ("misc", [128, 64], F32)):
        if tap in debug_taps:
            dbg[tap] = nc.dram_tensor("dbg_" + tap, shape, dt_, kind="ExternalOutput").ap()

    def at(name, shape, dtype, off):
        return nc.alloc_sbuf_tensor_at(name, shape, dtype, offset=off)

    BASE = 16512
    xres = at("xres", [128, WT * D], F32, BASE)
    arena = at("arena", [128, DT * W], BF16, 86144)
    mx2b = at("mx2b", [128, W], BF16, 86144)
    hrl = [at("hrl0", [128, 256], F32, 90752),
           at("hrl1", [128, 256], F32, 91776)]
    hql = at("hql", [128, 256], F32, 92800)
    ub2 = at("ub2", [128, 256], F16, 93824)
    hq2 = [at("hq2_0", [128, 256], BF16, 94336),
           at("hq2_1", [128, 256], BF16, 94848)]
    f3st = at("f3st", [128, D], F32, 95360)
    bigq = at("bigq", [128, JT * D], BF16, 120960)
    w1g = [at(f"w1g{s}", [128, 4 * 1024], BF16, 120960 + 8192 * s)
           for s in range(4)]
    p3a = at("p3a", [128, D], F32, 120960)
    p3b = at("p3b", [128, D], F32, 125056)
    y2bf = at("y2bf", [128, D], BF16, 129152)
    wf = [at("wf0", [128, DH], F32, 86144),
          at("wf1", [128, DH], F32, 102528)]
    wclip = at("wclip", [128, DH], F32, 120960)
    w1qs = [at("w1qs0", [128, DH], BF16, 137344),
            at("w1qs1", [128, DH], BF16, 145536)]
    w2qt = [at("w2qt0", [128, D], BF16, 153728),
            at("w2qt1", [128, D], BF16, 155776)]
    accw = at("accw", [128, 2], F32, 157824)
    redw = at("redw", [128, 4], F32, 157856)
    zedge = at("zedge", [HALO, D], BF16, 157952)
    splb = at("splb", [128, 128], BF16, 160000)
    tpadP = at("tpadP", [128, 128], BF16, 160256)
    mwrow = at("mwrow", [1, 16], F32, 160512)
    lvrow = at("lvrow", [1, 768], BF16, 160640)
    h1 = [at("h1_0", [128, 1090], F32, 186496),
          at("h1_1", [128, 1090], F32, 190976)]
    hcv = at("hcv", [128, 1090], F32, 195456)
    ytile = at("ytile", [128, D], F32, 186496)
    ubf = at("ubf", [128, D], F16, 190592)
    xqr = at("xqr", [128, D], BF16, 192640)
    abc3 = [at(f"abc3_{k}", [128, W], BF16, 186496 + 4352 * k) for k in range(3)]
    mx2 = at("mx2", [128, W], F32, 199936)
    vpadB = [at(f"vpadB{k}", [128, 128], BF16, 199936 + 256 * k) for k in range(3)]
    tpadB = at("tpadB", [128, 384], BF16, 200704)
    bsc = at("bsc", [128, 40], F32, 201536)
    abc = at("abc", [128, W], F32, 208640)
    a1bc = abc
    s2bc = abc
    hcv2 = at("hcv2", [128, 1090], F32, 217344)
    vbuf = at("vbuf", [128, (N // 128) * D], BF16, BASE)
    qt = at("qt", [128, N], BF16, 147584)
    kt = at("kt", [128, N], BF16, 163968)
    wvb = at("wvb", [128, DT * D], BF16, 180352)
    y2c = [at("y2c0", [128, DT * 512], BF16, 196736),
           at("y2c1", [128, DT * 512], BF16, 204928)]
    wqb = at("wqb", [128, DT * 128], BF16, 213120)
    wkb = at("wkb", [128, DT * 128], BF16, 215168)
    pstage = at("pstage", [128, 2 * D], BF16, 217216)
    maskA = at("maskA_t", [128, 256], BF16, 221824)
    maskB = at("maskB_t", [128, 256], BF16, 222336)
    simb = [at("simb0", [128, 256], BF16, 222848),
            at("simb1", [128, 256], BF16, 223360)]
    identb = at("identb_t", [128, 128], BF16, 226560)
    cwt = at("cwt", [128, JT * 3], F32, 226816)
    cbt = at("cbt", [128, JT], F32, 227200)
    rmask = at("rmask", [128, WT], F32, 227328)
    mwb = at("mwb", [128, 4], F32, 227968)
    a1v = at("a1v", [128, WT], F32, 228000)
    a2v = at("a2v", [128, WT], F32, 228096)
    s2v = at("s2v", [128, WT], F32, 228192)
    stat = at("stat", [128, 16], F32, 228288)
    cvec = at("cvec", [128, 2], F32, 228352)
    axr = at("axr", [128, WT], F32, 228416)
    ps_a = nc.alloc_psum_tensor("ps_a", [128, 512], F32)
    ps_b = nc.alloc_psum_tensor("ps_b", [128, 512], F32)
    ps_t = [nc.alloc_psum_tensor(f"ps_t{i}", [128, 1024], BF16) for i in range(2)]
    ps_f = [nc.alloc_psum_tensor(f"ps_f{i}", [128, 512], F32) for i in range(4)]

    with (
        nc.semaphore("dX") as _1, nc.semaphore("dW") as _2,
        nc.semaphore("dH") as _3, nc.semaphore("dHr") as _4,
        nc.semaphore("dS") as _5, nc.semaphore("dP") as _6,
        nc.semaphore("dO") as _7, nc.semaphore("dY") as _8,
        nc.semaphore("sq") as _9, nc.semaphore("sv") as _10,
        nc.semaphore("sm") as _11, nc.semaphore("se") as _12,
        nc.semaphore("sg") as _13, nc.semaphore("sb") as _14,
        nc.semaphore("sv2") as _15, nc.semaphore("sq2") as _16,
        nc.semaphore("dG") as _17, nc.semaphore("dW2") as _18,
        nc.Block() as block,
    ):
        dX, dW, dH, dHr = Sem(_1), Sem(_2), Sem(_3), Sem(_4)
        dS, dP, dO, dY = Sem(_5), Sem(_6), Sem(_7), Sem(_8)
        sq, sv, sm, se = Sem(_9), Sem(_10), Sem(_11), Sem(_12)
        sg, sb, sv2, sq2 = Sem(_13), Sem(_14), Sem(_15), Sem(_16)
        dG, dW2 = Sem(_17), Sem(_18)

        def vsep(v):
            v.tensor_scalar(stat[:, 15:16], stat[:, 0:1], 1.0, None, OP.mult)

        def barrier():
            sb.n += 5
            lvl = sb.n
            for name in ("tensor", "vector", "scalar", "gpsimd", "sync"):
                sec = getattr(block, name)

                def body(e, lvl=lvl):
                    e.nop().then_inc(sb.h, 1)
                    e.wait_ge(sb.h, lvl)
                sec(body)

        def emit_bcast(vec, dst, gate_list=()):
            @block.vector
            def _(v):
                v.memset(vpadB[0][:], 0.0)
                v.memset(vpadB[1][:], 0.0)
                v.memset(vpadB[2][:], 0.0)
                v.tensor_scalar(vpadB[0][:, 0:WT], vec[:, 0:WT], 1.0, None, OP.mult)
                vsep(v)
                v.tensor_tensor(bsc[:, 0:WT], vec[:, 0:WT], vpadB[0][:, 0:WT], OP.subtract)
                vsep(v)
                v.tensor_scalar(vpadB[1][:, 0:WT], bsc[:, 0:WT], 1.0, None, OP.mult)
                vsep(v)
                v.tensor_tensor(bsc[:, 20:20 + WT], bsc[:, 0:WT], vpadB[1][:, 0:WT],
                                OP.subtract)
                vsep(v)
                sv.inc(v.tensor_scalar(vpadB[2][:, 0:WT], bsc[:, 20:20 + WT],
                                       1.0, None, OP.mult))

            dT0 = dS.n

            @block.sync
            def _(s, dT0=dT0, sv_n=sv.n):
                s.wait_ge(sv.h, sv_n)
                for k in range(3):
                    dS.dinc(s.dma_start_transpose(out=tpadB[:, k * 128:(k + 1) * 128],
                                                  in_=vpadB[k][:]))
                s.wait_ge(dS.h, dT0 + 48)
                dS.dinc(s.dma_start(out=bc_sp, in_=tpadB[0:WT, 0:384]))
                s.wait_ge(dS.h, dT0 + 64)
                for sem, val in gate_list:
                    s.wait_ge(sem.h, val)
                for k in range(3):
                    dS.dinc(s.dma_start(
                        out=abc3[k][:, 0:W].rearrange("p (t u) -> p t u", t=WT),
                        in_=bc_sp[:, k * 128:(k + 1) * 128].unsqueeze(0)
                            .broadcast_to([128, WT, 128])))

            @block.vector
            def _(v, dn=dS.n):
                v.wait_ge(dS.h, dn)
                v.tensor_tensor(dst[:], abc3[0][:], abc3[1][:], OP.add)
                sv.inc(v.tensor_tensor(dst[:], dst[:], abc3[2][:], OP.add))

        def emit_ln(cols):
            xt = xres[:, cols]

            @block.scalar
            def _(sc):
                sc.wait_ge(sv.h, sv.n)
                sq.inc(sc.activation(ytile[:], xt, ACT.Square, accum_out=stat[:, 1:2]))

            @block.vector
            def _(v):
                v.tensor_reduce(stat[:, 0:1], xt, AX, OP.add)
                v.wait_ge(sq.h, sq.n)
                v.tensor_scalar(stat[:, 3:4], stat[:, 1:2], 1.0 / D, None, OP.mult)
                v.tensor_scalar(stat[:, 2:3], stat[:, 0:1], 1.0 / D, None, OP.mult)
                v.tensor_scalar(stat[:, 5:6], stat[:, 3:4], 1.0, 1e-5, OP.mult, OP.add)
                v.tensor_tensor(stat[:, 4:5], stat[:, 2:3], stat[:, 2:3], OP.mult)
                vsep(v)
                v.tensor_tensor(stat[:, 6:7], stat[:, 5:6], stat[:, 4:5], OP.subtract)
                vsep(v)
                sv.inc(v.reciprocal(stat[:, 7:8], stat[:, 6:7]))

            @block.scalar
            def _(sc):
                sc.wait_ge(sv.h, sv.n)
                sq.inc(sc.activation(stat[:, 8:9], stat[:, 7:8], ACT.Sqrt))

            @block.vector
            def _(v):
                v.wait_ge(sq.h, sq.n)
                v.tensor_tensor(stat[:, 9:10], stat[:, 8:9], stat[:, 8:9], OP.mult)
                v.tensor_scalar(stat[:, 13:14], stat[:, 2:3], -1.0, None, OP.mult)
                v.tensor_tensor(stat[:, 10:11], stat[:, 9:10], stat[:, 6:7], OP.mult)
                vsep(v)
                v.tensor_scalar(stat[:, 11:12], stat[:, 10:11], -0.5, 1.5, OP.mult, OP.add)
                vsep(v)
                v.tensor_tensor(stat[:, 12:13], stat[:, 8:9], stat[:, 11:12], OP.mult)
                vsep(v)
                sv.inc(v.tensor_tensor(stat[:, 14:15], stat[:, 13:14], stat[:, 12:13],
                                       OP.mult))

        def emit_final_store():
            @block.sync
            def _(s):
                src_hi = xres[64:128, 0:16 * D].rearrange("p (t c) -> p t c", c=D)
                dst_hi = out_d.rearrange("(t u) c -> u t c", u=128)[0:64, :, :]
                dO.dinc(s.dma_start(out=dst_hi, in_=src_hi))
                src_lo = xres[0:64, D:WT * D].rearrange("p (t c) -> p t c", c=D)
                dst_lo = out_d.rearrange("(t u) c -> u t c", u=128)[64:128, :, :]
                dO.dinc(s.dma_start(out=dst_lo, in_=src_lo))
                s.wait_ge(dO.h, dO.n)
            barrier()

        def emit_dbg_misc():
            if "misc" not in dbg:
                return

            @block.sync
            def _(s):
                s.wait_ge(sv.h, sv.n)
                s.wait_ge(dS.h, dS.n)
                dO.dinc(s.dma_start(out=dbg["misc"][:, 0:4], in_=mwb[:]))
                dO.dinc(s.dma_start(out=dbg["misc"][:, 4:4 + WT], in_=a1v[:]))
                dO.dinc(s.dma_start(out=dbg["misc"][:, 24:24 + WT], in_=a2v[:]))
                dO.dinc(s.dma_start(out=dbg["misc"][:, 44:44 + WT], in_=s2v[:]))
                dO.dinc(s.dma_start(out=dbg["misc"][:, 62:64], in_=accw[:]))

        @block.sync
        def _(s):
            dX.dinc(s.dma_start(out=identb[:], in_=identb_d))
            dX.dinc(s.dma_start(out=cwt[:], in_=cw_d))
            dX.dinc(s.dma_start(out=cbt[:], in_=cb_d))
            dX.dinc(s.dma_start(out=rmask[:], in_=rmask_d))
            dX.dinc(s.dma_start(out=xres[:].rearrange("p (t d) -> p t d", d=D),
                                in_=xw_d.rearrange("(t p) d -> p t d", p=128)))

        @block.vector
        def _(v):
            v.memset(accw[:], 0.0)
            v.memset(stat[:], 0.0)
            v.memset(cvec[:, 0:1], -MAGIC)
            v.memset(cvec[:, 1:2], RELU_SCALE)
            v.memset(splb[:], 0.0)

        @block.gpsimd
        def _(g):
            g.memset(zedge[:], 0.0)
            sg.inc(g.nop())

        @block.sync
        def _(s):
            s.wait_ge(sg.h, sg.n)
            for r_ in (0, 4):
                dS.dinc(s.dma_start(out=rs_in[r_, 0:HALO, :], in_=zedge[:]))
            for r_ in (3, 7):
                dS.dinc(s.dma_start(out=rs_in[r_, W - HALO:W, :], in_=zedge[:]))

        if stop_after == "consts":
            @block.sync
            def _(s):
                s.wait_ge(dX.h, dX.n)
                s.wait_ge(dS.h, dS.n)
            barrier()
            emit_final_store()
            return nc, dbg

        NW1, NW2 = DT, JT
        dW_p1 = dW.n

        @block.sync
        def _(s, sv_p1=sv.n):
            s.wait_ge(dX.h, dX.n)
            for k in range(NW1 + NW2):
                if k >= 2:
                    s.wait_ge(sv.h, sv_p1 + k - 1)
                if k < NW1:
                    dW.dinc(s.dma_start(out=wf[k % 2][:], in_=w1t_d[k * 128:(k + 1) * 128, :]))
                else:
                    i = k - NW1
                    dW.dinc(s.dma_start(out=wf[k % 2][:, 0:D], in_=w2t_d[i * 128:(i + 1) * 128, :]))

        @block.vector
        def _(v):
            K = NW1 + NW2
            for k in range(K + 1):
                if k < K:
                    v.wait_ge(dW.h, dW_p1 + (k + 1) * 16)
                    col = 0 if k < NW1 else 1
                    src = wf[k % 2][:] if k < NW1 else wf[k % 2][:, 0:D]
                    v.tensor_reduce(redw[:, k % 2:k % 2 + 1], src, AX, OP.add,
                                    apply_absolute_value=True)
                if k >= 1:
                    if k == K:
                        vsep(v)
                    pcol = 0 if (k - 1) < NW1 else 1
                    ins = v.tensor_tensor(accw[:, pcol:pcol + 1], accw[:, pcol:pcol + 1],
                                          redw[:, (k - 1) % 2:(k - 1) % 2 + 1], OP.add)
                    if k == K:
                        sv.inc(ins)
                    else:
                        sv.n += 1
                        ins.then_inc(sv.h, 1)

        @block.vector
        def _(v):
            v.tensor_scalar(splb[:, 0:1], accw[:, 0:1], 1.0, None, OP.mult)
            v.tensor_scalar(splb[:, 3:4], accw[:, 1:2], 1.0, None, OP.mult)
            v.tensor_tensor(redw[:, 0:1], accw[:, 0:1], splb[:, 0:1], OP.subtract)
            v.tensor_tensor(redw[:, 1:2], accw[:, 1:2], splb[:, 3:4], OP.subtract)
            v.tensor_scalar(splb[:, 1:2], redw[:, 0:1], 1.0, None, OP.mult)
            v.tensor_scalar(splb[:, 4:5], redw[:, 1:2], 1.0, None, OP.mult)
            v.tensor_tensor(redw[:, 2:3], redw[:, 0:1], splb[:, 1:2], OP.subtract)
            v.tensor_tensor(redw[:, 3:4], redw[:, 1:2], splb[:, 4:5], OP.subtract)
            v.tensor_scalar(splb[:, 2:3], redw[:, 2:3], 1.0, None, OP.mult)
            sv.inc(v.tensor_scalar(splb[:, 5:6], redw[:, 3:4], 1.0, None, OP.mult))

        dT0 = dS.n

        @block.sync
        def _(s, dT0=dT0, sv_n=sv.n):
            s.wait_ge(sv.h, sv_n)
            dS.dinc(s.dma_start_transpose(out=tpadP[:], in_=splb[:]))
            s.wait_ge(dS.h, dT0 + 16)
            dS.dinc(s.dma_start(out=mw_sp, in_=tpadP[0:6, 0:128]))
            s.wait_ge(dS.h, dT0 + 32)
            dS.dinc(s.dma_start(
                out=lvrow[0:1, 0:768].rearrange("p (t u) -> p t u", t=6),
                in_=mw_sp.unsqueeze(0)))

        @block.vector
        def _(v, dn=dS.n):
            lv = lvrow
            v.wait_ge(dS.h, dn)
            for c in range(6):
                v.tensor_reduce(mwrow[0:1, c:c + 1], lv[0:1, c * 128:(c + 1) * 128],
                                AX, OP.add)
            v.tensor_tensor(mwrow[0:1, 6:7], mwrow[0:1, 0:1], mwrow[0:1, 1:2], OP.add)
            v.tensor_tensor(mwrow[0:1, 7:8], mwrow[0:1, 3:4], mwrow[0:1, 4:5], OP.add)
            v.tensor_tensor(mwrow[0:1, 8:9], mwrow[0:1, 6:7], mwrow[0:1, 2:3], OP.add)
            v.tensor_tensor(mwrow[0:1, 9:10], mwrow[0:1, 7:8], mwrow[0:1, 5:6], OP.add)
            v.tensor_scalar(mwrow[0:1, 10:11], mwrow[0:1, 8:9], 1.0 / (DH * D), 1e-5,
                            OP.mult, OP.max)
            v.tensor_scalar(mwrow[0:1, 11:12], mwrow[0:1, 9:10], 1.0 / (DH * D), 1e-5,
                            OP.mult, OP.max)
            v.reciprocal(mwrow[0:1, 12:13], mwrow[0:1, 10:11])
            v.reciprocal(mwrow[0:1, 13:14], mwrow[0:1, 11:12])
            v.tensor_scalar(mwrow[0:1, 14:15], mwrow[0:1, 10:11], 1.0 / 127.0, None, OP.mult)
            sv.inc(v.tensor_scalar(mwrow[0:1, 15:16], mwrow[0:1, 11:12], 0.5 / 127.0,
                                   None, OP.mult))

        @block.sync
        def _(s, sv_n=sv.n, dn=dS.n):
            s.wait_ge(sv.h, sv_n)
            dS.dinc(s.dma_start(out=mwd, in_=mwrow[0:1, 12:16]))
            s.wait_ge(dS.h, dn + 16)
            dS.dinc(s.dma_start(out=mwb[:], in_=mwd[0:1, :].broadcast_to([128, 4])))

        dS_mwb = dS.n

        sv_p2, dW_p2 = sv.n, dW.n
        dS_w1 = dS.n

        @block.sync
        def _(s):
            for i in range(NW1 + 2):
                if 0 <= i - 2 < NW1:
                    s.wait_ge(sv.h, sv_p2 + (i - 2) + 1)
                    dS.dinc(s.dma_start(out=w1q_sp[:, (i - 2) * DH:(i - 1) * DH],
                                        in_=w1qs[(i - 2) % 2][:]))
                if i < NW1:
                    dW.dinc(s.dma_start(out=wf[i % 2][:], in_=w1t_d[i * 128:(i + 1) * 128, :]))

        @block.vector
        def _(v):
            v.wait_ge(dS.h, dS_mwb)
            for i in range(NW1):
                v.wait_ge(dW.h, dW_p2 + (i + 1) * 16)
                if i >= 2:
                    v.wait_ge(dS.h, dS_w1 + 16 * (i - 1))
                v.tensor_scalar(wclip[:], wf[i % 2][:], mwb[:, 0:1], -1.0, OP.mult, OP.max)
                for h4 in range(4):
                    sl = slice(h4 * D, (h4 + 1) * D)
                    v.tensor_scalar(ubf[:], wclip[:, sl], 1.0, MAGIC, OP.min, OP.add)
                    ins = v.tensor_scalar(w1qs[i % 2][:, sl], ubf[:], MAGIC, None, OP.subtract)
                sv.inc(ins)

        sv_p3, dW_p3, dS_p3 = sv.n, dW.n, dS.n

        @block.sync
        def _(s):
            for i in range(NW2 + 2):
                if 0 <= i - 2 < NW2:
                    s.wait_ge(sv.h, sv_p3 + (i - 2) + 1)
                    dS.dinc(s.dma_start(out=w2q_sp[:, (i - 2) * D:(i - 1) * D],
                                        in_=w2qt[(i - 2) % 2][:]))
                if i < NW2:
                    dW.dinc(s.dma_start(out=wf[i % 2][:, 0:D],
                                        in_=w2t_d[i * 128:(i + 1) * 128, :]))

        @block.vector
        def _(v):
            for i in range(NW2):
                v.wait_ge(dW.h, dW_p3 + (i + 1) * 16)
                if i >= 2:
                    v.wait_ge(dS.h, dS_p3 + 16 * (i - 1))
                v.tensor_scalar(wclip[:, 0:D], wf[i % 2][:, 0:D], mwb[:, 1:2], -1.0,
                                OP.mult, OP.max)
                v.tensor_scalar(ubf[:], wclip[:, 0:D], 1.0, MAGIC, OP.min, OP.add)
                sv.inc(v.tensor_scalar(w2qt[i % 2][:], ubf[:], MAGIC, None, OP.subtract))

        @block.sync
        def _(s):
            s.wait_ge(dS.h, dS.n)
            s.wait_ge(dX.h, dX.n)
        barrier()
        if stop_after == "p0":
            emit_dbg_misc()
            emit_final_store()
            return nc, dbg

        first_mm = [True]

        def emit_ffn(phase):
            final = (phase == "C")
            fstop = stop_after if phase == "A" else None

            for t in range(WT):
                cols = slice(t * D, (t + 1) * D)
                emit_ln(cols)

                @block.scalar
                def _(sc, cols=cols):
                    sc.wait_ge(sv.h, sv.n)
                    sq.inc(sc.activation(ytile[:], xres[:, cols], ACT.Identity,
                                         bias=stat[:, 14:15], scale=stat[:, 12:13]))

                @block.vector
                def _(v, t=t):
                    v.wait_ge(sq.h, sq.n)
                    v.tensor_reduce(stat[:, 8:9], ytile[:], AX, OP.max,
                                    apply_absolute_value=True)
                    vsep(v)
                    v.tensor_scalar(stat[:, 9:10], stat[:, 8:9], 1.0, 1e-5, OP.mult, OP.max)
                    vsep(v)
                    v.reciprocal(stat[:, 10:11], stat[:, 9:10])
                    v.tensor_tensor(a1v[:, t:t + 1], stat[:, 9:10], mwb[:, 2:3], OP.mult)
                    v.tensor_scalar(stat[:, 11:12], stat[:, 10:11], 127.0, None, OP.mult)
                    vsep(v)
                    v.tensor_scalar(ubf[:], ytile[:], stat[:, 11:12], MAGIC, OP.mult, OP.add)
                    sv.inc(v.tensor_scalar(xqr[:], ubf[:], MAGIC, None, OP.subtract))

                for k in range(DT):
                    pst = ps_t[k % 2]

                    @block.tensor
                    def _(te, k=k, pst=pst):
                        if k == 0:
                            if not first_mm[0]:
                                te.nop(cycle_cnt=PAD_TILE)
                            te.wait_ge(sv.h, sv.n)
                        if sq2.n >= 1:
                            te.wait_ge(sq2.h, sq2.n - 1)
                        sm.inc(te.transpose(pst[:, 0:128], xqr[:, k * 128:(k + 1) * 128],
                                            identb[:]))

                    @block.scalar
                    def _(sc, t=t, k=k, pst=pst):
                        sc.wait_ge(sm.h, sm.n)
                        sq2.inc(sc.copy(arena[:, k * W + t * 128:k * W + (t + 1) * 128],
                                        pst[:, 0:128]))

            if fstop == "f0":
                return
            emit_bcast(a1v, a1bc, gate_list=((sq2, sq2.n),))
            if fstop == "f0b":
                return

            @block.vector
            def _(v):
                v.memset(mx2[:], 0.0)
                v.memset(hcv[:], 0.0)
                v.memset(hcv2[:], 0.0)
                v.memset(h1[0][:], 0.0)
                sv.inc(v.memset(h1[1][:], 0.0))

            HB = [0, 1086]
            CH_H = [(0, 512), (512, 290), (802, 288)]
            NCH = len(CH_H)
            NSTEP = JT * 2
            sm_f1, se_f1, sv_f1, sq_f1 = sm.n, se.n, sv.n, sq.n
            dH_f1, dG_f1, sv2_f1 = dH.n, dG.n, sv2.n
            dW2_base = dW2.n

            @block.gpsimd
            def _(g, sv_pre=sv.n, sm_f1=sm_f1):
                for q in range(8):
                    if q == 0:
                        g.wait_ge(sv.h, sv_pre)
                    if q >= 4:
                        g.wait_ge(sm.h, sm_f1 + (q - 3) * 8 * NCH)
                    for i in range(DT):
                        dG.dinc(g.dma_start(
                            out=w1g[q % 4][:, i * 512:(i + 1) * 512],
                            in_=w1q_sp[:, i * DH + q * 512:i * DH + (q + 1) * 512]))

            @block.scalar
            def _(sc, sv_n=sv.n, dg=dG_f1 + 128, sq2_n=sq2.n):
                sc.wait_ge(sv.h, sv_n)
                sc.wait_ge(dG.h, dg)
                sc.wait_ge(sq2.h, sq2_n)
                sq.inc(sc.nop())

            sq_gate = sq.n

            @block.tensor
            def _(te):
                for st in range(NSTEP):
                    j, hf = st // 2, st % 2
                    q = j // 4
                    te.nop(cycle_cnt=PAD_F1ST)
                    for ci, (c0, cwd) in enumerate(CH_H):
                        step = st * NCH + ci
                        pst = (ps_a, ps_b)[step % 2]
                        if step == 0:
                            te.wait_ge(sq.h, sq_gate)
                        elif step >= 2:
                            te.wait_ge(se.h, se_f1 + step - 1)
                        for i in range(DT):
                            mm_ = te.matmul(pst[:, 0:cwd],
                                            w1g[q % 4][:, i * 512 + (j % 4) * 128:
                                                       i * 512 + (j % 4) * 128 + 128],
                                            arena[:, i * W + HB[hf] + c0:
                                                  i * W + HB[hf] + c0 + cwd],
                                            start=(i == 0), stop=(i == DT - 1))
                        sm.inc(mm_)
            first_mm[0] = False

            @block.vector
            def _(v):
                for st in range(NSTEP):
                    j, hf = st // 2, st % 2
                    hj = h1[st % 2]
                    if st >= 2:
                        v.wait_ge(sq.h, sq_f1 + 1 + 2 * (st - 1))
                        v.wait_ge(dH.h, dH_f1 + 16 * (st - 1))
                    for ci, (c0, cwd) in enumerate(CH_H):
                        step = st * NCH + ci
                        pst = (ps_a, ps_b)[step % 2]
                        if (step + 2) % 24 == 0 and step + 2 > 0:
                            qn = (step + 2) // 24
                            if qn <= 7:
                                v.wait_ge(dG.h, dG_f1 + 128 * (qn + 1))
                        v.wait_ge(sm.h, sm_f1 + step + 1)
                        se.inc(v.tensor_tensor(hj[:, c0:c0 + cwd], pst[:, 0:cwd],
                                               a1bc[:, HB[hf] + c0:HB[hf] + c0 + cwd],
                                               OP.mult))
                    if st >= 1:
                        v.wait_ge(sq.h, sq_f1 + 2 * (st - 1) + 3)
                        phf = (st - 1) % 2
                        m0 = 0 if phf == 0 else 1088
                        rl = 0 if phf == 0 else 2
                        v.tensor_tensor(mx2[:, m0:m0 + 1088], mx2[:, m0:m0 + 1088],
                                        hcv2[:, rl:rl + 1088], OP.max)
                    v.wait_ge(sq.h, sq_f1 + 1 + 2 * st + 1)
                    v.scalar_tensor_tensor(hcv2[:, 1:1089], hj[:, 1:1089],
                                           cwt[:, 3 * j + 1:3 * j + 2], hcv[:, 1:1089],
                                           OP.mult, OP.add)
                    sv2.inc(v.scalar_tensor_tensor(hcv[:, 1:1089], hj[:, 2:1090],
                                                   cwt[:, 3 * j + 2:3 * j + 3],
                                                   hcv2[:, 1:1089], OP.mult, OP.add))

            @block.scalar
            def _(sc):
                for st in range(NSTEP):
                    j = st // 2
                    sc.wait_ge(se.h, se_f1 + NCH * (st + 1))
                    if st >= 1:
                        sc.wait_ge(sv2.h, sv2_f1 + st)
                    hj = h1[st % 2]
                    sq.inc(sc.activation(hcv[:, 1:1089], hj[:, 0:1088], ACT.Identity,
                                         scale=cwt[:, 3 * j:3 * j + 1],
                                         bias=cbt[:, j:j + 1]))
                    sc.wait_ge(sv2.h, sv2_f1 + st + 1)
                    hf = st % 2
                    rl = 0 if hf == 0 else 2
                    sc.activation(hj[:, rl:rl + 1088], hcv[:, rl:rl + 1088], ACT.Silu)
                    sq.inc(sc.activation(hcv2[:, rl:rl + 1088], hj[:, rl:rl + 1088],
                                         ACT.Abs))

            @block.sync
            def _(s):
                for st in range(NSTEP):
                    j, hf = st // 2, st % 2
                    s.wait_ge(sq.h, sq_f1 + 1 + 2 * st + 2)
                    if hf == 0:
                        dH.dinc(s.dma_start(out=h_sp[j, :, 0:1088],
                                            in_=h1[st % 2][:, 0:1088]))
                    else:
                        dH.dinc(s.dma_start(out=h_sp[j, :, 1088:2176],
                                            in_=h1[st % 2][:, 2:1090]))
                    if 1 <= st <= 16:
                        c = 15 + st
                        dW2.dinc(s.dma_start(out=bigq[:, c * D:(c + 1) * D],
                                             in_=w2q_sp[:, c * D:(c + 1) * D]))
                    if st in (38, 46, 54, 62):
                        sl = (st - 38) // 8
                        s.wait_ge(sm.h, sm_f1 + 8 * NCH * (sl + 5))
                        for c in range(4 * sl, 4 * sl + 4):
                            dW2.dinc(s.dma_start(out=bigq[:, c * D:(c + 1) * D],
                                                 in_=w2q_sp[:, c * D:(c + 1) * D]))

            @block.vector
            def _(v):
                st = NSTEP - 1
                v.wait_ge(sq.h, sq_f1 + 1 + 2 * st + 2)
                phf = st % 2
                m0 = 0 if phf == 0 else 1088
                rl = 0 if phf == 0 else 2
                sv.inc(v.tensor_tensor(mx2[:, m0:m0 + 1088], mx2[:, m0:m0 + 1088],
                                       hcv2[:, rl:rl + 1088], OP.max))

            if fstop == "f1":
                return
            @block.vector
            def _(v):
                v.wait_ge(sv.h, sv.n)
                sv.inc(v.tensor_scalar(mx2b[:], mx2[:], 1.0, None, OP.mult))

            se_f2, sm_f2 = se.n, sm.n

            @block.tensor
            def _(te, sv_n=sv.n):
                te.nop(cycle_cnt=PAD_F2IN)
                te.wait_ge(sv.h, sv_n)
                for t in range(WT):
                    if t >= 2:
                        te.wait_ge(se.h, se_f2 + t - 1)
                    sm.inc(te.transpose(ps_t[t % 2][:, 0:128],
                                        mx2b[:, t * 128:(t + 1) * 128], identb[:]))

            @block.vector
            def _(v):
                for t in range(WT):
                    v.wait_ge(sm.h, sm_f2 + t + 1)
                    se.inc(v.tensor_reduce(axr[:, t:t + 1], ps_t[t % 2][:, 0:128],
                                           AX, OP.max))
                vsep(v)
                v.tensor_scalar(bsc[:, 20:20 + WT], axr[:], 1.0, 1e-5, OP.mult, OP.max)
                vsep(v)
                v.reciprocal(bsc[:, 0:WT], bsc[:, 20:20 + WT])
                v.tensor_scalar(a2v[:], bsc[:, 20:20 + WT], mwb[:, 3:4], None, OP.mult)
                sv.inc(v.tensor_scalar(s2v[:], bsc[:, 0:WT], 127.0, None, OP.mult))

            emit_bcast(s2v, s2bc)

            if fstop == "f2":
                return
            def w2_ready(j, base=dW2_base):
                idx = (j - 15) if j >= 16 else (17 + j)
                return base + 16 * idx

            for g in range(9):
                gc0 = g * 256
                gw = 256 if g < 8 else 128
                nsub = 2 if g < 8 else 1
                dHr_g, sv2_g, sq2_g, sm_g = dHr.n, sv2.n, sq2.n, sm.n
                dH_all3 = dH.n

                @block.sync
                def _(s, g=g, gc0=gc0, gw=gw, sv2_g=sv2_g, dH_all3=dH_all3):
                    if g == 0:
                        s.wait_ge(dH.h, dH_all3)
                    for j in range(JT):
                        if sv2_g + j - 1 > 0:
                            s.wait_ge(sv2.h, sv2_g + j - 1)
                        dHr.dinc(s.dma_start(out=hrl[j % 2][:, 0:gw],
                                             in_=h_sp[j, :, gc0:gc0 + gw]))

                @block.vector
                def _(v, gc0=gc0, gw=gw, dHr_g=dHr_g, sq2_g=sq2_g):
                    for j in range(JT):
                        v.wait_ge(dHr.h, dHr_g + 16 * (j + 1))
                        if sq2_g + j > 0:
                            v.wait_ge(sq2.h, sq2_g + j)
                        v.tensor_tensor(hql[:, 0:gw], hrl[j % 2][:, 0:gw],
                                        s2bc[:, gc0:gc0 + gw], OP.mult)
                        sv2.inc(v.tensor_scalar(ub2[:, 0:gw], hql[:, 0:gw], 1.0, MAGIC,
                                                OP.mult, OP.add))

                @block.scalar
                def _(sc, gw=gw, sv2_g=sv2_g, sm_g=sm_g):
                    for j in range(JT):
                        sc.wait_ge(sv2.h, sv2_g + j + 1)
                        sc.wait_ge(dW2.h, w2_ready(j))
                        if sm_g + j - 1 > 0:
                            sc.wait_ge(sm.h, sm_g + j - 1)
                        sq2.inc(sc.activation(hq2[j % 2][:, 0:gw], ub2[:, 0:gw],
                                              ACT.Identity, bias=cvec[:, 0:1], scale=1.0))

                @block.tensor
                def _(te, gw=gw, nsub=nsub, sq2_g=sq2_g):
                    te.nop(cycle_cnt=PAD_F3G)
                    for j in range(JT):
                        te.wait_ge(sq2.h, sq2_g + j + 1)
                        if j == 0:
                            te.wait_ge(se.h, se.n)
                        for a in range(nsub):
                            for vc in range(2):
                                mm_ = te.matmul(ps_f[2 * a + vc][:],
                                                hq2[j % 2][:, a * 128:a * 128 + 128],
                                                bigq[:, j * D + vc * 512:j * D + (vc + 1) * 512],
                                                start=(j == 0), stop=(j == JT - 1))
                        sm.inc(mm_)

                @block.vector
                def _(v, g=g, nsub=nsub):
                    v.wait_ge(sm.h, sm.n)
                    for a in range(nsub):
                        t = 2 * g + a
                        for vc in range(2):
                            v.scalar_tensor_tensor(
                                f3st[:, vc * 512:(vc + 1) * 512],
                                ps_f[2 * a + vc][:], a2v[:, t:t + 1],
                                xres[:, t * D + vc * 512:t * D + (vc + 1) * 512],
                                OP.mult, OP.add)
                        ins = v.tensor_scalar(xres[:, t * D:(t + 1) * D], f3st[:],
                                              rmask[:, t:t + 1], None, OP.mult)
                    se.inc(ins)

        emit_ffn("A")
        if stop_after in ("f0", "f0b", "f1", "f2", "ffnA"):
            if stop_after == "ffnA" and "x1" in dbg:
                @block.sync
                def _(s):
                    s.wait_ge(se.h, se.n)
                    dS.dinc(s.dma_start(out=dbg["x1"], in_=xres[:]))
            emit_dbg_misc()
            emit_final_store()
            return nc, dbg

        if "x1" in dbg:
            @block.sync
            def _(s):
                s.wait_ge(se.h, se.n)
                dS.dinc(s.dma_start(out=dbg["x1"], in_=xres[:]))

        @block.vector
        def _(v):
            v.wait_ge(se.h, se.n)
            sv.inc(v.nop())

        for t in range(WT):
            cols = slice(t * D, (t + 1) * D)
            emit_ln(cols)

            @block.scalar
            def _(sc, cols=cols):
                sc.wait_ge(sv.h, sv.n)
                sq.inc(sc.activation(y2bf[:], xres[:, cols], ACT.Identity,
                                     bias=stat[:, 14:15], scale=stat[:, 12:13]))

            for k in range(DT):
                pst = ps_t[k % 2]

                @block.tensor
                def _(te, k=k, pst=pst):
                    if k == 0:
                        te.nop(cycle_cnt=PAD_TILE)
                        te.wait_ge(sq.h, sq.n)
                    te.wait_ge(sq2.h, sq2.n - 1)
                    sm.inc(te.transpose(pst[:, 0:128], y2bf[:, k * 128:(k + 1) * 128],
                                        identb[:]))

                @block.scalar
                def _(sc, t=t, k=k, pst=pst):
                    sc.wait_ge(sm.h, sm.n)
                    sq2.inc(sc.copy(arena[:, k * W + t * 128:k * W + (t + 1) * 128],
                                    pst[:, 0:128]))

            @block.vector
            def _(v):
                v.wait_ge(sq2.h, sq2.n)
                sv.inc(v.nop())

        @block.sync
        def _(s):
            s.wait_ge(sq2.h, sq2.n)
            for k in range(DT):
                dS.dinc(s.dma_start(out=ag_in[k * 128:(k + 1) * 128, :],
                                    in_=arena[:, k * W + HALO:k * W + HALO + R]))
            dS.dinc(s.dma_start(out=x1_sp, in_=xres[:]))
            if "y2t" in dbg:
                for k in range(DT):
                    dS.dinc(s.dma_start(out=dbg["y2t"][k * 128:(k + 1) * 128, :],
                                        in_=arena[:, k * W + HALO:k * W + HALO + R]))

        if stop_after == "aend":
            emit_final_store()
            return nc, dbg

        @block.gpsimd
        def _(g):
            g.wait_ge(dS.h, dS.n)
            sg.inc(g.collective_compute("AllGather", OP.bypass, [list(range(NCORE))],
                                        [ag_in[:]], [ag_out[:]]))

        if stop_after == "ag":
            emit_final_store()
            return nc, dbg
        @block.tensor
        def _(te):
            te.nop(cycle_cnt=PAD_AG)

        @block.sync
        def _(s):
            s.wait_ge(sg.h, sg.n)
            dW.dinc(s.dma_start(out=maskA[:], in_=maskA_d))
            dW.dinc(s.dma_start(out=maskB[:], in_=maskB_d))

        @block.gpsimd
        def _(g):
            for i in range(DT):
                dG.dinc(g.dma_start(out=wqb[:, i * 128:(i + 1) * 128],
                                    in_=wqh_d[i * 128:(i + 1) * 128, :]))
                dG.dinc(g.dma_start(out=wkb[:, i * 128:(i + 1) * 128],
                                    in_=wkh_d[i * 128:(i + 1) * 128, :]))
                dG.dinc(g.dma_start(out=wvb[:, i * D:(i + 1) * D],
                                    in_=wvh_d[i * 128:(i + 1) * 128, :]))

        for b in range(B):
            sm_qkv, se_qkv, dY_b = sm.n, se.n, dY.n

            @block.sync
            def _(s, b=b, sm_qkv=sm_qkv, dY_b=dY_b):
                s.wait_ge(sg.h, sg.n)
                s.wait_ge(dW.h, dW.n)
                s.wait_ge(dG.h, dG.n)
                for ch in range(16):
                    rk = 4 * b + ch // 4
                    loc = (ch % 4) * 512
                    if ch >= 2:
                        s.wait_ge(sm.h, sm_qkv + (ch - 1) * 10)
                    elif b == 1:
                        s.wait_ge(sm.h, sm_qkv)
                    dY.dinc(s.dma_start(
                        out=y2c[ch % 2][:].rearrange("p (i c) -> p i c", c=512),
                        in_=ag_out[rk, :, loc:loc + 512].rearrange("(i p) c -> p i c", p=128)))

            @block.tensor
            def _(te, sm_qkv=sm_qkv, se_qkv=se_qkv, dY_b=dY_b):
                for ch in range(16):
                    te.wait_ge(dY.h, dY_b + 16 * (ch + 1))
                    ych = y2c[ch % 2]
                    if ch >= 1:
                        te.wait_ge(se.h, se_qkv + 10 * ch)
                    else:
                        te.wait_ge(se.h, se_qkv)
                    for i in range(DT):
                        mm_ = te.matmul(ps_a[:], wqb[:, i * 128:(i + 1) * 128],
                                        ych[:, i * 512:(i + 1) * 512],
                                        start=(i == 0), stop=(i == DT - 1))
                    sm.inc(mm_)
                    for i in range(DT):
                        mm_ = te.matmul(ps_b[:], wkb[:, i * 128:(i + 1) * 128],
                                        ych[:, i * 512:(i + 1) * 512],
                                        start=(i == 0), stop=(i == DT - 1))
                    sm.inc(mm_)
                    for ns in range(4):
                        if ns >= 2:
                            te.wait_ge(se.h, se_qkv + 10 * ch + 2 + 2 * (ns - 1))
                        for vc in range(2):
                            pf = ps_f[(ns % 2) * 2 + vc]
                            for i in range(DT):
                                mm_ = te.matmul(pf[:],
                                                ych[:, i * 512 + ns * 128:i * 512 + ns * 128 + 128],
                                                wvb[:, i * D + vc * 512:i * D + (vc + 1) * 512],
                                                start=(i == 0), stop=(i == DT - 1))
                            sm.inc(mm_)

            @block.scalar
            def _(sc, sm_qkv=sm_qkv):
                for ch in range(16):
                    base = sm_qkv + 10 * ch
                    sc.wait_ge(sm.h, base + 1)
                    se.inc(sc.copy(qt[:, ch * 512:(ch + 1) * 512], ps_a[:]))
                    sc.wait_ge(sm.h, base + 2)
                    se.inc(sc.copy(kt[:, ch * 512:(ch + 1) * 512], ps_b[:]))
                    for ns in range(4):
                        for vc in range(2):
                            pf = ps_f[(ns % 2) * 2 + vc]
                            sc.wait_ge(sm.h, base + 2 + ns * 2 + vc + 1)
                            nt = ch * 4 + ns
                            se.inc(sc.copy(vbuf[:, nt * D + vc * 512:nt * D + (vc + 1) * 512],
                                           pf[:]))

            for j in range(NCHUNK):
                nkt = 2 * j + 2
                sm_ch, se_ch, sq_ch, sv2_ch = sm.n, se.n, sq.n, sv2.n

                @block.tensor
                def _(te, j=j, nkt=nkt, sm_ch=sm_ch, se_ch=se_ch, sq_ch=sq_ch):
                    for i in range(nkt):
                        pst = (ps_a, ps_b)[i % 2]
                        te.wait_ge(se.h, se_ch + i - 1 if i >= 2 else se_ch)
                        sm.inc(te.matmul(pst[:, 0:256], kt[:, i * 128:(i + 1) * 128],
                                         qt[:, j * 256:(j + 1) * 256], start=True, stop=True))
                        te.wait_ge(sq.h, sq_ch + i + 1)
                        for a in range(2):
                            for vc in range(2):
                                mm_ = te.matmul(ps_f[2 * a + vc][:],
                                                simb[i % 2][:, a * 128:(a + 1) * 128],
                                                vbuf[:, i * D + vc * 512:i * D + (vc + 1) * 512],
                                                start=(i == 0), stop=(i == nkt - 1))
                        sm.inc(mm_)

                @block.scalar
                def _(sc, nkt=nkt, sm_ch=sm_ch):
                    for i in range(nkt):
                        pst = (ps_a, ps_b)[i % 2]
                        sc.wait_ge(sm.h, sm_ch + 2 * i + 1)
                        ins = sc.activation(simb[i % 2][:], pst[:, 0:256], ACT.Relu,
                                            scale=cvec[:, 1:2])
                        if i >= nkt - 2:
                            sv2.inc(ins)
                            se.inc(sc.activation(stat[:, 15:16], cvec[:, 0:1], ACT.Identity))
                        else:
                            sq.inc(ins)
                            se.inc(sc.activation(stat[:, 15:16], cvec[:, 0:1], ACT.Identity))

                @block.vector
                def _(v, nkt=nkt, sv2_ch=sv2_ch):
                    for di, i in enumerate((nkt - 2, nkt - 1)):
                        v.wait_ge(sv2.h, sv2_ch + di + 1)
                        mk = maskA if di == 0 else maskB
                        sq.inc(v.tensor_tensor(simb[i % 2][:], simb[i % 2][:], mk[:],
                                               OP.mult))

                dP_pre = dP.n

                @block.vector
                def _(v, nkt=nkt, sm_ch=sm_ch, dP_pre=dP_pre):
                    v.wait_ge(sm.h, sm_ch + 2 * nkt)
                    v.wait_ge(dP.h, dP_pre)
                    for a in range(2):
                        for vc in range(2):
                            ins = v.tensor_copy(
                                pstage[:, a * D + vc * 512:a * D + (vc + 1) * 512],
                                ps_f[2 * a + vc][:])
                            if (a, vc) == (1, 1):
                                sv.inc(ins)
                                se.inc(v.nop())
                            else:
                                se.inc(ins)

                @block.sync
                def _(s, b=b, j=j):
                    s.wait_ge(sv.h, sv.n)
                    g0 = 256 * j
                    for a in range(2):
                        blk0 = g0 + a * 128
                        for r_ in range(4 * b, 4 * b + 4):
                            w0 = 2048 * (r_ % 4) - HALO
                            lo = max(blk0, w0)
                            hi = min(blk0 + 128, w0 + W)
                            if lo >= hi:
                                continue
                            dP.dinc(s.dma_start(
                                out=rs_in[r_, lo - w0:hi - w0, :],
                                in_=pstage[(lo - blk0):(hi - blk0), a * D:(a + 1) * D]))

        @block.gpsimd
        def _(g):
            g.wait_ge(dP.h, dP.n)
            sg.inc(g.collective_compute("ReduceScatter", OP.add, [list(range(NCORE))],
                                        [rs_in[:]], [rs_out[:]]))

        @block.tensor
        def _(te):
            te.nop(cycle_cnt=PAD_RS)

        if stop_after == "rs":
            emit_final_store()
            return nc, dbg
        @block.sync
        def _(s):
            s.wait_ge(sg.h, sg.n)
            dX.dinc(s.dma_start(out=xres[:], in_=x1_sp))
            dX.dinc(s.dma_start(out=arena[:, 0:WT * D].rearrange("p (t d) -> p t d", d=D),
                                in_=rs_out.rearrange("(t p) d -> p t d", p=128)))
            if "hout" in dbg:
                dS.dinc(s.dma_start(out=dbg["hout"], in_=rs_out))

        @block.vector
        def _(v):
            v.wait_ge(dX.h, dX.n)
            sv.inc(v.nop())

        for t in range(WT):
            cols = slice(t * D, (t + 1) * D)
            emit_ln(cols)

            @block.scalar
            def _(sc, cols=cols):
                sc.wait_ge(sv.h, sv.n)
                sq.inc(sc.activation(ytile[:], xres[:, cols], ACT.Identity,
                                     bias=stat[:, 14:15], scale=stat[:, 12:13]))

            @block.vector
            def _(v, t=t, cols=cols):
                v.wait_ge(sq.h, sq.n)
                v.tensor_tensor(p3a[:], xres[:, cols], ytile[:], OP.add)
                v.tensor_tensor(p3b[:], p3a[:], arena[:, t * D:(t + 1) * D], OP.add)
                sv.inc(v.tensor_scalar(xres[:, cols], p3b[:], rmask[:, t:t + 1],
                                       None, OP.mult))

        if "x2" in dbg:
            @block.sync
            def _(s):
                s.wait_ge(sv.h, sv.n)
                dS.dinc(s.dma_start(out=dbg["x2"], in_=xres[:]))

        if stop_after == "p3":
            emit_final_store()
            return nc, dbg
        emit_ffn("C")

        @block.tensor
        def _(te):
            te.nop(cycle_cnt=PAD_FIN)

        @block.sync
        def _(s):
            s.wait_ge(se.h, se.n)
            src_hi = xres[64:128, 0:16 * D].rearrange("p (t c) -> p t c", c=D)
            dst_hi = out_d.rearrange("(t u) c -> u t c", u=128)[0:64, :, :]
            dO.dinc(s.dma_start(out=dst_hi, in_=src_hi))
            src_lo = xres[0:64, D:WT * D].rearrange("p (t c) -> p t c", c=D)
            dst_lo = out_d.rearrange("(t u) c -> u t c", u=128)[64:128, :, :]
            dO.dinc(s.dma_start(out=dst_lo, in_=src_lo))
            s.wait_ge(dO.h, dO.n)
        barrier()

    return nc, dbg


def _host_inputs(inputs):
    x = np.asarray(inputs["L0_attn_x"], np.float32).reshape(B * N, D)
    Wq = np.asarray(inputs["L0_attn_Wq"], np.float32)
    Wk = np.asarray(inputs["L0_attn_Wk"], np.float32)
    Wv = np.asarray(inputs["L0_attn_Wv"], np.float32)
    w1 = np.asarray(inputs["L0_ffn_w1"], np.float32)
    w2 = np.asarray(inputs["L0_ffn_w2"], np.float32)
    cwr = np.asarray(inputs["L0_ffn_conv_w"], np.float32).reshape(DH, 3)
    cbr = np.asarray(inputs["L0_ffn_conv_b"], np.float32).reshape(DH)

    w1t = np.ascontiguousarray(w1.T)
    w2t = np.ascontiguousarray(w2.T)
    cw = np.ascontiguousarray(cwr.reshape(JT, 128, 3).transpose(1, 0, 2).reshape(128, JT * 3))
    cb = np.ascontiguousarray(cbr.reshape(JT, 128).T)
    identb = np.eye(128, dtype=bfloat16)
    k_i = np.arange(128)[:, None]
    q_i = np.arange(256)[None, :]
    maskA = (((q_i % 128) >= k_i) | (q_i >= 128)).astype(bfloat16)
    maskB = ((q_i >= 128) & ((q_i % 128) >= k_i)).astype(bfloat16)

    in_maps = []
    for c in range(NCORE):
        b = c // 4
        lrow = 2048 * (c % 4)
        xwin = np.zeros((W, D), np.float32)
        lo = max(0, lrow - HALO)
        hi = min(N, lrow + R + HALO)
        xwin[(lo - (lrow - HALO)):(hi - (lrow - HALO))] = x[b * N + lo: b * N + hi]
        rmv = np.zeros((W,), np.float32)
        rmv[(lo - (lrow - HALO)):(hi - (lrow - HALO))] = 1.0
        rmaskv = np.ascontiguousarray(rmv.reshape(WT, 128).T)
        in_maps.append({
            "xw": xwin,
            "rowmask": rmaskv,
            "w1t": w1t, "w2t": w2t, "cw": cw, "cb": cb,
            "wqh": np.ascontiguousarray(Wq[c].T),
            "wkh": np.ascontiguousarray(Wk[c].T),
            "wvh": np.ascontiguousarray(Wv[c].T),
            "identb": identb,
            "maskA": maskA, "maskB": maskB,
        })
    return in_maps


def kernel(**inputs):
    in_maps = _host_inputs(inputs)
    nc, _ = build_program()
    res = run_bass_kernel_spmd(nc, in_maps, core_ids=list(range(NCORE)))
    outs = [r["out"] for r in res.results]
    return np.concatenate(outs, axis=0).reshape(B, N, D)
